# version 01
# baseline (speedup 1.0000x reference)
import numpy as np
import ml_dtypes
import concourse.bass as bass
import concourse.mybir as mybir
from concourse.bass_utils import run_bass_kernel_spmd
from concourse.alu_op_type import AluOpType as OP
from concourse.bass import broadcast_tensor_aps as _bcast

F32 = mybir.dt.float32
BF16 = mybir.dt.bfloat16
I8 = mybir.dt.int8
AF = mybir.ActivationFunctionType
AX = mybir.AxisListType

T = 4096
D = 2048
DC = D // 128
TT_ = 512
NT = T // TT_
BF = ml_dtypes.bfloat16
NEG = -1.0e30

ENGINES = ("sync", "scalar", "vector", "tensor", "gpsimd")


class Prog:

    def __init__(self, nc):
        self.nc = nc
        self.q = {e: [] for e in ENGINES}
        self.sem_names = []
        self.counts = {}
        self.guards = {}

    def sem(self, name):
        if name not in self.counts:
            self.sem_names.append(name)
            self.counts[name] = 0
        return name

    def inc(self, name, k=1):
        self.counts[name] += k
        return self.counts[name]

    def val(self, name):
        return self.counts[name]

    def on(self, engine):
        def deco(fn):
            self.q[engine].append(fn)
            return None
        return deco

    def guard(self, key):
        return self.guards.get(key)

    def set_guard(self, key, sem, val):
        self.guards[key] = (sem, val)

    def dma(self, qname, out, in_, sem, waits=(), **kw):
        v = self.inc(sem, 16)
        waits = [w for w in waits if w is not None]
        def f(eng, S):
            for (ws, wv) in waits:
                eng.wait_ge(S[ws], wv)
            eng.dma_start(out=out, in_=in_, **kw).then_inc(S[sem], 16)
        self.q[qname].append(f)
        return v

    def wait(self, qname, sem, v):
        def f(eng, S):
            eng.wait_ge(S[sem], v)
        self.q[qname].append(f)

    def op(self, qname, sem, waits, fn, selfwait=False):
        v = self.inc(sem, 1)
        waits = [w for w in waits if w is not None]
        if selfwait and v > 1:
            waits = waits + [(sem, v - 1)]
        def f(eng, S):
            for (ws, wv) in waits:
                eng.wait_ge(S[ws], wv)
            fn(eng, S).then_inc(S[sem], 1)
        self.q[qname].append(f)
        return v

    def alloc_sems(self, stack):
        S = {}
        for name in self.sem_names:
            S[name] = stack.enter_context(self.nc.semaphore(name))
        return S

    def run(self, block, S):
        @block.sync
        def _(e):
            for f in self.q["sync"]:
                f(e, S)

        @block.scalar
        def _(e):
            for f in self.q["scalar"]:
                f(e, S)

        @block.vector
        def _(e):
            for f in self.q["vector"]:
                f(e, S)

        @block.tensor
        def _(e):
            for f in self.q["tensor"]:
                f(e, S)

        @block.gpsimd
        def _(e):
            for f in self.q["gpsimd"]:
                f(e, S)


ARENA_N = 106368


class Carver:
    def __init__(self, arena):
        self.a = arena
        self.off = 0
        self.marks = {}

    def bf(self, name, *dims):
        n = int(np.prod(dims))
        ap = self.a[:, self.off:self.off + n]
        if len(dims) > 1:
            ap = ap.rearrange("p (a b) -> p a b", a=dims[0]) if len(dims) == 2 else \
                 ap.rearrange("p (a b c) -> p a b c", a=dims[0], b=dims[1])
        self.marks[name] = (self.off, n)
        self.off += n
        return ap

    def f32(self, name, *dims):
        n = int(np.prod(dims))
        ap = self.a[:, self.off:self.off + 2 * n].bitcast(F32)
        if len(dims) > 1:
            ap = ap.rearrange("p (a b) -> p a b", a=dims[0]) if len(dims) == 2 else \
                 ap.rearrange("p (a b c) -> p a b c", a=dims[0], b=dims[1])
        self.marks[name] = (self.off, 2 * n)
        self.off += 2 * n
        return ap

    def alias_bf(self, off_name, *dims):
        off = self.marks[off_name][0]
        n = int(np.prod(dims))
        ap = self.a[:, off:off + n]
        if len(dims) > 1:
            ap = ap.rearrange("p (a b) -> p a b", a=dims[0]) if len(dims) == 2 else \
                 ap.rearrange("p (a b c) -> p a b c", a=dims[0], b=dims[1])
        return ap



def emit_lnprep(p, C, tag, r_in, ti, rms=False):
    sb, pb = C.sb, C.pb
    ts, te = ti * TT_, (ti + 1) * TT_
    xs = p.sem(tag + "x")
    mm = C.mm; sc = C.sc; dv = C.dv

    xv = p.dma("gpsimd", sb["xbf"][:, :, :],
               C.tn[r_in].rearrange("c p t -> p c t")[:, :, ts:te],
               xs, waits=[p.guard("xbf")], max_dma_last_dim=8192)

    sum_bank, sq_bank = pb[0], pb[1]
    for c in range(DC):
        sqv = p.op("scalar", sc, [(xs, xv), p.guard(f"sq{c % 2}")],
                   lambda e, S, c=c: e.activation(sb["sq"][:, c % 2, :], sb["xbf"][:, c, :], AF.Square))
        p.op("tensor", mm, [(xs, xv), p.guard("pb0") if c == 0 else None],
             lambda e, S, c=c: e.matmul(sum_bank[:, :], sb["ones_bf"][:, :], sb["xbf"][:, c, :],
                                        start=(c == 0), stop=(c == DC - 1)))
        mv = p.op("tensor", mm, [(sc, sqv), p.guard("pb1") if c == 0 else None],
                  lambda e, S, c=c: e.matmul(sq_bank[:, :], sb["ones_bf"][:, :], sb["sq"][:, c % 2, :],
                                             start=(c == 0), stop=(c == DC - 1)))
        p.set_guard(f"sq{c % 2}", C.mm, mv)
    mmv = p.val(mm)

    bc = sb["bc"]
    p.op("vector", dv, [(mm, mmv)],
         lambda e, S: e.tensor_scalar(bc[:, 0, :], sum_bank[:, :], 1.0 / D, None, op0=OP.mult))
    p.op("vector", dv, [(mm, mmv)],
         lambda e, S: e.tensor_scalar(bc[:, 3, :], sq_bank[:, :], 1.0 / D, None, op0=OP.mult))
    b0v = p.val(dv)
    p.set_guard("pb0", dv, b0v)
    p.set_guard("pb1", dv, b0v)
    p.op("vector", dv, [],
         lambda e, S: e.tensor_tensor(bc[:, 4, :], bc[:, 0, :], bc[:, 0, :], op=OP.mult))
    dvv = p.op("vector", dv, [],
               lambda e, S: e.tensor_tensor(bc[:, 3, :], bc[:, 3, :], bc[:, 4, :], op=OP.subtract))
    scv = p.op("scalar", sc, [(dv, dvv)],
               lambda e, S: e.activation(bc[:, 4, :], bc[:, 3, :], AF.Sqrt, bias=sb["epsc"][:, 0:1]))
    p.op("vector", dv, [(sc, scv)],
         lambda e, S: e.reciprocal(bc[:, 1, :], bc[:, 4, :]))
    if rms:
        p.op("vector", dv, [],
             lambda e, S: e.tensor_tensor(bc[:, 4, :], bc[:, 3, :], bc[:, 1, :], op=OP.mult))
        dvv = p.op("vector", dv, [],
                   lambda e, S: e.tensor_tensor(bc[:, 4, :], bc[:, 4, :], bc[:, 1, :], op=OP.mult))
        scv = p.op("scalar", sc, [(dv, dvv)],
                   lambda e, S: e.activation(bc[:, 4, :], bc[:, 4, :], AF.Sqrt, bias=sb["epsc"][:, 1:2]))
        p.op("vector", dv, [(sc, scv)],
             lambda e, S: e.reciprocal(bc[:, 4, :], bc[:, 4, :]))
        p.op("vector", dv, [],
             lambda e, S: e.tensor_tensor(bc[:, 1, :], bc[:, 1, :], bc[:, 4, :], op=OP.mult))
    p.op("vector", dv, [],
         lambda e, S: e.tensor_tensor(bc[:, 2, :], bc[:, 0, :], bc[:, 1, :], op=OP.mult))

    for c in range(DC):
        p.op("vector", dv, [p.guard("xln") if c == 0 else None],
             lambda e, S, c=c: e.tensor_tensor(sb["t32"][:, 0, :], sb["xbf"][:, c, :], bc[:, 1, :], op=OP.mult))
        if rms:
            p.op("vector", dv, [],
                 lambda e, S, c=c: e.tensor_tensor(sb["t32"][:, 0, :], sb["t32"][:, 0, :], bc[:, 2, :], op=OP.subtract))
            p.op("vector", dv, [],
                 lambda e, S, c=c: e.tensor_scalar(sb["xln"][:, c, :], sb["t32"][:, 0, :], C.sb["rmsw"][:, c:c + 1], None, op0=OP.mult))
        else:
            p.op("vector", dv, [],
                 lambda e, S, c=c: e.tensor_tensor(sb["xln"][:, c, :], sb["t32"][:, 0, :], bc[:, 2, :], op=OP.subtract))
    return p.val(dv)



def emit_postln_residual(p, C, tag, r_in, r_out, ti, y_ready):
    sb, pb = C.sb, C.pb
    ts, te = ti * TT_, (ti + 1) * TT_
    mm, sc, dv = C.mm, C.sc, C.dv
    rs = p.sem(tag + "r")
    st = C.st
    bc = sb["bc"]
    ybf = sb["ybf"]

    for c in range(DC):
        sqv = p.op("scalar", sc, [y_ready, p.guard(f"sq{c % 2}")],
                   lambda e, S, c=c: e.activation(sb["sq"][:, c % 2, :], ybf[:, c, :], AF.Square))
        p.op("tensor", mm, [y_ready, p.guard("pb0") if c == 0 else None],
             lambda e, S, c=c: e.matmul(pb[0][:, :], sb["ones_bf"][:, :], ybf[:, c, :],
                                        start=(c == 0), stop=(c == DC - 1)))
        mv = p.op("tensor", mm, [(sc, sqv), p.guard("pb1") if c == 0 else None],
                  lambda e, S, c=c: e.matmul(pb[1][:, :], sb["ones_bf"][:, :], sb["sq"][:, c % 2, :],
                                             start=(c == 0), stop=(c == DC - 1)))
        p.set_guard(f"sq{c % 2}", C.mm, mv)
    mmv = p.val(mm)
    p.op("vector", dv, [(mm, mmv)],
         lambda e, S: e.tensor_scalar(bc[:, 5, :], pb[0][:, :], 1.0 / D, None, op0=OP.mult))
    p.op("vector", dv, [(mm, mmv)],
         lambda e, S: e.tensor_scalar(bc[:, 7, :], pb[1][:, :], 1.0 / D, None, op0=OP.mult))
    b0v = p.val(dv)
    p.set_guard("pb0", dv, b0v)
    p.set_guard("pb1", dv, b0v)
    p.op("vector", dv, [],
         lambda e, S: e.tensor_tensor(bc[:, 4, :], bc[:, 5, :], bc[:, 5, :], op=OP.mult))
    dvv = p.op("vector", dv, [],
               lambda e, S: e.tensor_tensor(bc[:, 7, :], bc[:, 7, :], bc[:, 4, :], op=OP.subtract))
    scv = p.op("scalar", sc, [(dv, dvv)],
               lambda e, S: e.activation(bc[:, 4, :], bc[:, 7, :], AF.Sqrt, bias=sb["epsc"][:, 0:1]))
    p.op("vector", dv, [(sc, scv)],
         lambda e, S: e.reciprocal(bc[:, 6, :], bc[:, 4, :]))
    p.op("vector", dv, [],
         lambda e, S: e.tensor_tensor(bc[:, 7, :], bc[:, 5, :], bc[:, 6, :], op=OP.mult))

    for c in range(DC):
        b = c % 2
        rv = p.dma("sync", sb["rbuf"][:, b, :],
                   C.tn[r_in][c][:, ts:te], rs, waits=[p.guard(f"rbuf{b}")])
        p.op("vector", dv, [(rs, rv)],
             lambda e, S, c=c, b=b: e.tensor_tensor(sb["t32"][:, 0, :], ybf[:, c, :], bc[:, 6, :], op=OP.mult))
        p.op("vector", dv, [],
             lambda e, S, c=c, b=b: e.tensor_tensor(sb["t32"][:, 1, :], sb["rbuf"][:, b, :], bc[:, 7, :], op=OP.subtract))
        rb_free = p.op("vector", dv, [p.guard(f"rnew{b}")],
                       lambda e, S, c=c, b=b: e.tensor_tensor(sb["rnew"][:, b, :], sb["t32"][:, 0, :], sb["t32"][:, 1, :], op=OP.add))
        p.set_guard(f"rbuf{b}", C.dv, rb_free)
        sv = p.dma("sync", C.tn[r_out][c][:, ts:te], sb["rnew"][:, b, :], st,
                   waits=[(C.dv, rb_free)])
        p.set_guard(f"rnew{b}", st, sv)
    return p.val(dv)



def emit_ffn(p, C, cfg):
    sb, pb, tn = C.sb, C.pb, C.tn
    tag = cfg["tag"]
    mm, sc, dv, st = C.mm, C.sc, C.dv, C.st
    ws = p.sem(tag + "w")
    NIC = cfg["n_ic"]
    routed = cfg["routed"]
    g_cache, u_cache, d_cache = tn[cfg["gcache"]], tn[cfg["ucache"]], tn[cfg["dcache"]]
    PIECE = 16
    npieces = (NIC + PIECE - 1) // PIECE

    xs = p.sem(tag + "prep")
    for (cache, host, n_oc, row) in cfg["preps"]:
        step = max(1, (8 * 2048 * 128) // (row * 128))
        oc = 0
        while oc < n_oc:
            hi = min(n_oc, oc + step)
            p.dma("gpsimd", tn[cache][oc:hi], tn[host][oc:hi], xs, max_dma_last_dim=cfg.get("prep_mdld", 8192))
            oc += hi - oc
    prep_done = p.val(xs)
    if routed:
        p.dma("gpsimd", sb["rc"][:, :], tn["router_h"], xs)
        prep_done = p.val(xs)

    for ti in range(NT):
        ts, te = ti * TT_, (ti + 1) * TT_
        xln_ready = emit_lnprep(p, C, tag, cfg["r_in"], ti, rms=False)
        xr = (C.dv, xln_ready)

        comb_ready = None
        if routed:
            lgv = None
            for dc in range(DC):
                lgv = p.op("tensor", mm, [xr, (xs, prep_done), p.guard("pb0") if dc == 0 else None],
                           lambda e, S, dc=dc: e.matmul(pb[0][0:8, :], sb["rc"][:, dc * 8:(dc + 1) * 8], sb["xln"][:, dc, :],
                                                        start=(dc == 0), stop=(dc == DC - 1)))
            ev = p.op("vector", dv, [(mm, lgv)],
                      lambda e, S: e.tensor_copy(sb["lg"][0:8, :], pb[0][0:8, :]))
            p.set_guard("pb0", dv, ev)
            for q in range(4):
                tv = p.op("tensor", mm, [(dv, ev), p.guard("pb1")],
                          lambda e, S, q=q: e.transpose(pb[1][:, 0:8], sb["lg"][0:8, q * 128:(q + 1) * 128], sb["ident32"][0:8, 0:8]))
                rows = sb["rows"]
                p.op("vector", dv, [(mm, tv)],
                     lambda e, S: e.reduce_max(rows[:, 0:1], pb[1][:, 0:8], axis=AX.X), selfwait=True)
                p.op("vector", dv, [],
                     lambda e, S: e.tensor_scalar(sb["rows2"][:, 0:8], pb[1][:, 0:8], rows[:, 0:1], None, op0=OP.is_ge), selfwait=True)
                p.op("vector", dv, [],
                     lambda e, S: e.reduce_sum(rows[:, 1:2], sb["rows2"][:, 0:8], axis=AX.X), selfwait=True)
                sttv = p.op("vector", dv, [],
                     lambda e, S: e.scalar_tensor_tensor(sb["rows2"][:, 0:8], sb["rows2"][:, 0:8], NEG, pb[1][:, 0:8], op0=OP.mult, op1=OP.add), selfwait=True)
                p.set_guard("pb1", dv, sttv)
                bkv = p.op("vector", dv, [],
                           lambda e, S: e.reduce_max(rows[:, 2:3], sb["rows2"][:, 0:8], axis=AX.X), selfwait=True)
                p.op("vector", dv, [],
                     lambda e, S: e.tensor_scalar(rows[:, 3:4], rows[:, 1:2], 1.5, None, op0=OP.is_gt), selfwait=True)
                p.op("vector", dv, [],
                     lambda e, S: e.tensor_tensor(rows[:, 4:5], rows[:, 0:1], rows[:, 2:3], op=OP.subtract), selfwait=True)
                p.op("vector", dv, [],
                     lambda e, S: e.scalar_tensor_tensor(rows[:, 4:5], rows[:, 4:5], rows[:, 3:4], rows[:, 2:3], op0=OP.mult, op1=OP.add), selfwait=True)
                dvv = p.op("vector", dv, [],
                           lambda e, S: e.tensor_tensor(rows[:, 5:6], rows[:, 4:5], rows[:, 0:1], op=OP.subtract), selfwait=True)
                scv = p.op("scalar", sc, [(dv, dvv)],
                           lambda e, S: e.activation(rows[:, 6:7], rows[:, 5:6], AF.Exp))
                p.op("vector", dv, [(sc, scv)],
                     lambda e, S: e.tensor_scalar(rows[:, 6:7], rows[:, 6:7], 1.0, None, op0=OP.add), selfwait=True)
                p.op("vector", dv, [],
                     lambda e, S: e.reciprocal(rows[:, 7:8], rows[:, 6:7]), selfwait=True)
                cbv = p.op("vector", dv, [],
                           lambda e, S: e.tensor_tensor(rows[:, 0:1], rows[:, 6:7], rows[:, 7:8], op=OP.mult), selfwait=True)
                tv2 = p.op("tensor", mm, [(dv, cbv), p.guard("pb1")],
                           lambda e, S, q=q: e.transpose(pb[1][0:1, 0:128], rows[:, 0:1], sb["ident32"][:, :]))
                bkv = p.op("vector", dv, [(mm, tv2)],
                           lambda e, S, q=q: e.tensor_copy(sb["combrow"][0:1, q * 128:(q + 1) * 128], pb[1][0:1, 0:128]))
                p.set_guard("pb1", dv, bkv)
            cbfv = p.op("vector", dv, [],
                        lambda e, S: e.tensor_copy(sb["combrow_bf"][0:1, :], sb["combrow"][0:1, :]))
            kv = p.op("tensor", mm, [(dv, cbfv), p.guard("pb1")],
                      lambda e, S: e.matmul(pb[1][:, :], sb["ones_bf"][0:1, :], sb["combrow_bf"][0:1, :],
                                            start=True, stop=True))
            comb_ready = p.op("vector", dv, [(mm, kv)],
                              lambda e, S: e.tensor_copy(sb["comb32"][:, :], pb[1][:, :]))
            p.set_guard("pb1", dv, comb_ready)

        for ic in range(NIC):
            b = ic % 2
            gp, up_ = pb[2 + b], pb[4 + b]
            wav = p.dma("sync", sb["wA"][:, b, :], g_cache[ic], ws,
                        waits=[p.guard(f"wA{b}"), (xs, prep_done)])
            wbv = p.dma("sync", sb["wB"][:, b, :], u_cache[ic], ws,
                        waits=[p.guard(f"wB{b}"), (xs, prep_done)])
            gmv = umv = None
            for dc in range(DC):
                gmv = p.op("tensor", mm, [(ws, wav), xr, p.guard(f"pb{2+b}") if dc == 0 else None],
                           lambda e, S, dc=dc, b=b, gp=gp: e.matmul(gp[:, :], sb["wA"][:, b, dc * 128:(dc + 1) * 128], sb["xln"][:, dc, :],
                                                                   start=(dc == 0), stop=(dc == DC - 1)))
            p.set_guard(f"wA{b}", mm, gmv)
            for dc in range(DC):
                umv = p.op("tensor", mm, [(ws, wbv), p.guard(f"pb{4+b}") if dc == 0 else None],
                           lambda e, S, dc=dc, b=b, up_=up_: e.matmul(up_[:, :], sb["wB"][:, b, dc * 128:(dc + 1) * 128], sb["xln"][:, dc, :],
                                                                     start=(dc == 0), stop=(dc == DC - 1)))
            p.set_guard(f"wB{b}", mm, umv)
            if cfg["int8"]:
                gev = p.op("scalar", sc, [(mm, gmv), p.guard(f"g{b}")],
                           lambda e, S, b=b, ic=ic, gp=gp: e.activation(sb["g"][:, b, :], gp[:, :], AF.Silu, scale=sb["sgate"][:, ic:ic + 1]))
                uev = p.op("scalar", sc, [(mm, umv), p.guard(f"u{b}")],
                           lambda e, S, b=b, ic=ic, up_=up_: e.activation(sb["u"][:, b, :], up_[:, :], AF.Copy, scale=sb["sup"][:, ic:ic + 1]))
            else:
                gev = p.op("scalar", sc, [(mm, gmv), p.guard(f"g{b}")],
                           lambda e, S, b=b, gp=gp: e.activation(sb["g"][:, b, :], gp[:, :], AF.Silu))
                uev = p.op("scalar", sc, [(mm, umv), p.guard(f"u{b}")],
                           lambda e, S, b=b, up_=up_: e.activation(sb["u"][:, b, :], up_[:, :], AF.Copy))
            p.set_guard(f"pb{2+b}", sc, gev)
            p.set_guard(f"pb{4+b}", sc, uev)
            if routed and ic >= 64:
                uev2 = p.op("vector", dv, [(sc, uev), (dv, comb_ready)],
                            lambda e, S, b=b: e.tensor_tensor(sb["u"][:, b, :], sb["u"][:, b, :], sb["comb32"][:, :], op=OP.mult))
                fw = [(sc, gev), (dv, uev2)]
            else:
                fw = [(sc, gev), (sc, uev)]
            fv = p.op("vector", dv, fw + [p.guard("f") if ic == 0 else None],
                      lambda e, S, b=b, ic=ic: e.tensor_tensor(sb["f"][:, ic, :], sb["g"][:, b, :], sb["u"][:, b, :], op=OP.mult))
            p.set_guard(f"g{b}", dv, fv)
            p.set_guard(f"u{b}", dv, fv)
        f_ready = p.val(dv)
        p.set_guard("xln", mm, p.val(mm))
        if ti == 0 and "dbg_xln" in tn:
            p.dma("sync", tn["dbg_xln"], sb["xln"][:, :, :], C.st, waits=[xr])
            if comb_ready is not None:
                p.dma("sync", tn["dbg_comb"], sb["comb32"][:, :], C.st, waits=[(dv, comb_ready)])
                p.dma("sync", tn["dbg_lg"], sb["lg"][:, :], C.st, waits=[(dv, comb_ready)])
            p.dma("sync", tn["dbg_f"], sb["f"][:, :, :], C.st, waits=[(dv, f_ready)])

        y_done = None
        for hc in range(DC):
            b = hc % 2
            dpb = pb[6 + b]
            dmv = None
            for pc in range(npieces):
                lo, hi = pc * PIECE, min(NIC, (pc + 1) * PIECE)
                pbuf = C.misc.setdefault(tag + "_wCpiece", 0) % 2
                C.misc[tag + "_wCpiece"] += 1
                wv = p.dma("sync", sb["wC"][:, pbuf, 0:(hi - lo) * 128],
                           d_cache[hc][:, lo * 128:hi * 128], ws,
                           waits=[p.guard(f"wCp{pbuf}"), (xs, prep_done)])
                for ic in range(lo, hi):
                    dmv = p.op("tensor", mm,
                               [(ws, wv), (dv, f_ready), p.guard(f"pb{6+b}") if ic == 0 else None],
                               lambda e, S, ic=ic, lo=lo, pbuf=pbuf, dpb=dpb: e.matmul(
                                   dpb[:, :], sb["wC"][:, pbuf, (ic - lo) * 128:(ic - lo + 1) * 128], sb["f"][:, ic, :],
                                   start=(ic == 0), stop=(ic == NIC - 1)))
                p.set_guard(f"wCp{pbuf}", mm, dmv)
            if cfg["int8"]:
                yv = p.op("scalar", sc, [(mm, dmv), p.guard("ybf") if hc == 0 else None],
                          lambda e, S, hc=hc, dpb=dpb: e.activation(sb["ybf"][:, hc, :], dpb[:, :], AF.Copy, scale=sb["sdown"][:, hc:hc + 1]))
            else:
                yv = p.op("scalar", sc, [(mm, dmv), p.guard("ybf") if hc == 0 else None],
                          lambda e, S, hc=hc, dpb=dpb: e.activation(sb["ybf"][:, hc, :], dpb[:, :], AF.Copy))
            p.set_guard(f"pb{6+b}", sc, yv)
            y_done = yv
        p.set_guard("f", mm, p.val(mm))

        if ti == 0 and "dbg_xln" in tn:
            p.dma("sync", tn["dbg_y"], sb["ybf"][:, :, :], C.st, waits=[(sc, y_done)])
        res_done = emit_postln_residual(p, C, tag, cfg["r_in"], cfg["r_out"], ti, (sc, y_done))
        p.set_guard("xbf", C.dv, res_done)



def emit_ingest(p, C):
    sb, tn = C.sb, C.tn
    x, r0 = tn["x_in"], tn["R0"]
    ld = p.sem("ig_ld"); tp = p.sem("ig_tp"); ev = p.sem("ig_ev"); st = C.st
    xin, xout, ident = sb["ing_in"], sb["ing_out"], sb["ident32"]
    pbanks = C.pb[0:4]
    st0 = p.val(st)
    ntt = T // 128
    tp0 = p.val(tp); ev0 = p.val(ev)
    for i in range(ntt):
        buf = i % 2
        if i >= 2:
            p.wait("sync", tp, tp0 + DC * (i - 1))
        ldv = p.dma("sync", xin[:, buf, :], x[i * 128:(i + 1) * 128, :], ld)
        for c in range(DC):
            k = i * DC + c
            pb = pbanks[k % 4]
            tv = p.op("tensor", tp, [(ld, ldv), (ev, ev0 + k - 3) if k >= 4 else None],
                      lambda e, S, pb=pb, buf=buf, c=c: e.transpose(pb[:, 0:128], xin[:, buf, c * 128:(c + 1) * 128], ident[:, :]))
            p.op("vector", ev, [(tp, tv), (st, st0 + 16 * (i - 1)) if i >= 2 else None],
                 lambda e, S, pb=pb, buf=buf, c=c: e.tensor_copy(xout[:, buf, c * 128:(c + 1) * 128], pb[:, 0:128]))
        evv = p.val(ev)
        p.inc(st, 16)
        @p.on("sync")
        def _(e, S, evv=evv, buf=buf, i=i):
            e.wait_ge(S[ev], evv)
            e.dma_start(out=r0.rearrange("c p t -> p c t")[:, :, i * 128:(i + 1) * 128],
                        in_=xout[:, buf, :].rearrange("p (c x) -> p c x", c=DC)).then_inc(S[st], 16)
    for b in range(4):
        p.set_guard(f"bank{b}" if b < 2 else f"bankG{b-2}", ev, p.val(ev))
    p.set_guard("ing", st, p.val(st))


def emit_egress(p, C, rname):
    sb, tn = C.sb, C.tn
    r, out = tn[rname], tn["out"]
    ld = p.sem("eg_ld"); tp = p.sem("eg_tp"); ev = p.sem("eg_ev"); st = C.st
    xin, xout, ident = sb["ing_in"], sb["ing_out"], sb["ident32"]
    pbanks = C.pb[0:4]
    ntt = T // 128
    tp0 = p.val(tp); ev0 = p.val(ev); st0 = p.val(st)
    for i in range(ntt):
        buf = i % 2
        if i >= 2:
            p.wait("sync", tp, tp0 + DC * (i - 1))
        ldv = p.dma("sync", xin[:, buf, :].rearrange("p (c x) -> p c x", c=DC),
                    r.rearrange("c p t -> p c t")[:, :, i * 128:(i + 1) * 128], ld,
                    waits=[p.guard("ing") if i == 0 else None])
        for c in range(DC):
            k = i * DC + c
            pb = pbanks[k % 4]
            tv = p.op("tensor", tp, [(ld, ldv), (ev, ev0 + k - 3) if k >= 4 else None,
                                     p.guard(f"pb{k}") if k < 4 else None],
                      lambda e, S, pb=pb, buf=buf, c=c: e.transpose(pb[:, 0:128], xin[:, buf, c * 128:(c + 1) * 128], ident[:, :]))
            p.op("vector", ev, [(tp, tv), (st, st0 + 16 * (i - 1)) if i >= 2 else None],
                 lambda e, S, pb=pb, buf=buf, c=c: e.tensor_copy(xout[:, buf, c * 128:(c + 1) * 128], pb[:, 0:128]))
        evv = p.val(ev)
        p.inc(st, 16)
        @p.on("sync")
        def _(e, S, evv=evv, buf=buf, i=i):
            e.wait_ge(S[ev], evv)
            e.dma_start(out=out[i * 128:(i + 1) * 128, :], in_=xout[:, buf, :]).then_inc(S[st], 16)


class NS:
    pass


def build_program(stages, io_kinds):
    nc = bass.Bass("TRN2", target_bir_lowering=False)
    tn = {}

    def dram(name, shape, dtype):
        kind = io_kinds.get(name)
        if kind is None:
            return None
        tn[name] = nc.dram_tensor(name, shape, dtype, kind=kind).ap()
        return tn[name]

    dram("x_in", [T, D], F32)
    dram("out", [T, D], F32)
    for rn in ("R0", "R1", "R2", "R3", "R4"):
        dram(rn, [DC, 128, T], F32)
    dram("ident32_c", [128, 128], F32)
    dram("ones_bf_c", [128, 128], BF16)
    dram("eps_c", [128, 4], F32)
    dram("dbg_xln", [128, DC, 512], BF16)
    dram("dbg_f", [128, 72, 512], BF16)
    dram("dbg_y", [128, DC, 512], BF16)
    dram("dbg_comb", [128, 512], F32)
    dram("dbg_lg", [128, 512], F32)
    dram("wg_h", [72, 128, 2048], F32)
    dram("wu_h", [72, 128, 2048], F32)
    dram("wd_h", [16, 128, 9216], F32)
    dram("router_h", [128, 128], F32)
    dram("wg_c", [72, 128, 2048], BF16)
    dram("wu_c", [72, 128, 2048], BF16)
    dram("wd_c", [16, 128, 9216], BF16)
    dram("w1g_h", [64, 128, 2048], I8)
    dram("w1u_h", [64, 128, 2048], I8)
    dram("w1d_h", [16, 128, 8192], I8)
    dram("w1g_c", [64, 128, 2048], BF16)
    dram("w1u_c", [64, 128, 2048], BF16)
    dram("w1d_c", [16, 128, 8192], BF16)
    dram("sgate_h", [128, 64], F32)
    dram("sup_h", [128, 64], F32)
    dram("sdown_h", [128, 16], F32)
    dram("identbf_c", [128, 128], BF16)
    for n_ in ("q0", "k0", "v0", "o0"):
        dram(n_, [DC, 128, T], BF16)
    dram("wqkv_h", [48, 128, 2048], F32)
    dram("wqkv_c", [48, 128, 2048], BF16)
    dram("wcomb_h", [128, 1536], F32)
    dram("rmsw_h", [128, DC], F32)
    dram("bcomb_h", [96, 1], F32)
    dram("wout_h", [16, 128, 2048], F32)
    dram("wout_c", [16, 128, 2048], BF16)
    dram("kcw6_h", [32, 64, 4096], F32)
    dram("vcw6_h", [32, 64, 4096], F32)
    dram("kposT_h", [32, 64, 64], F32)
    dram("vposT_h", [32, 64, 64], F32)
    dram("kcb_h", [32, 64], F32)
    dram("vcb_h", [32, 64], F32)
    dram("memkT_h", [32, 64, 4], F32)
    dram("memv_h", [32, 4, 64], F32)
    dram("cmask_h", [68, T], F32)
    dram("lmask_h", [128, 256], F32)
    dram("q1", [DC, 128, T], BF16)
    dram("k1", [8, 128, T], BF16)
    dram("v1tm", [32, 128, 1024], BF16)
    dram("o1", [DC, 128, T], BF16)
    dram("wqkvtm_h", [16, 128, 4096], F32)
    dram("wqkvtm_c", [16, 128, 4096], BF16)
    dram("wout1_h", [16, 128, 2048], F32)
    dram("wout1_c", [16, 128, 2048], BF16)
    dram("costm_h", [128, 2048], F32)
    dram("sintm_h", [128, 2048], F32)
    dram("causal01_h", [128, 128], F32)
    dram("stair_h", [128, 3, 512], F32)
    dram("fmasktm_h", [128, 32, 32], F32)
    dram("dbg_ckx", [68, 68], BF16)
    dram("dbg_cvx", [68, 68], BF16)
    dram("dbg_loc", [128, 32, 65], BF16)
    dram("dbg_cmp", [128, 32, 65], BF16)
    dram("dbg_nrm", [128, 17], F32)
    dram("dbg_otm", [128, 32, 64], BF16)
    dram("dbg_csim", [68, 128], F32)
    dram("dbg_ptc", [68, 128], BF16)
    dram("dbg_cwtm", [128, 32, 96], BF16)
    dram("dbg_nb", [128, 96], F32)
    dram("dbg_cw", [96, T], F32)
    dram("dbg_cw2", [96, T], F32)

    import contextlib
    with contextlib.ExitStack() as stack:
        arena = stack.enter_context(nc.sbuf_tensor("arena", [128, ARENA_N], BF16))
        cv = Carver(arena)
        sb = {}
        sb["ones_bf"] = cv.bf("ones_bf", 128)
        sb["ident32"] = cv.f32("ident32", 128)
        sb["bc"] = cv.f32("bc", 8, 512)
        sb["t32"] = cv.f32("t32", 2, 512)
        sb["rows"] = cv.f32("rows", 8)
        sb["rows2"] = cv.f32("rows2", 8)
        sb["epsc"] = cv.f32("epsc", 4)
        sb["rbuf"] = cv.f32("rbuf", 2, 512)
        sb["rnew"] = cv.f32("rnew", 2, 512)
        sb["lg"] = cv.f32("lg", 512)
        sb["combrow"] = cv.f32("combrow", 512)
        sb["comb32"] = cv.f32("comb32", 512)
        sb["combrow_bf"] = cv.bf("combrow_bf", 512)
        sb["ing_in"] = cv.f32("ing_in", 2, 2048)
        sb["ing_out"] = cv.f32("ing_out", 2, 2048)
        sb["xbf"] = cv.bf("xbf", DC, 512)
        sb["xln"] = cv.bf("xln", DC, 512)
        sb["ybf"] = cv.alias_bf("xbf", DC, 512)
        sb["sq"] = cv.bf("sq", 2, 512)
        sb["g"] = cv.bf("g", 2, 512)
        sb["u"] = cv.bf("u", 2, 512)
        sb["wA"] = cv.bf("wA", 2, 2048)
        sb["wB"] = cv.bf("wB", 2, 2048)
        sb["wC"] = cv.bf("wC", 2, 2048)
        sb["rc"] = cv.bf("rc", 128)
        sb["sgate"] = cv.f32("sgate", 64)
        sb["sup"] = cv.f32("sup", 64)
        sb["sdown"] = cv.f32("sdown", 16)
        sb["f"] = cv.bf("f", 72, 512)
        sb["identbf"] = cv.bf("identbf", 128)
        sb["lmask"] = cv.bf("lmask", 256)
        sb["nrm"] = cv.f32("nrm", 17)
        sb["nb"] = cv.f32("nb", 96)
        sb["vcbrow"] = cv.f32("vcbrow", 128)
        sb["vcbrow_bf"] = cv.bf("vcbrow_bf", 64)
        sb["kcb"] = cv.f32("kcb", 2)
        sb["bcomb"] = cv.f32("bcomb", 1)
        sb["rmsw"] = cv.f32("rmsw", DC)
        wcoff = cv.marks["wC"][0]
        sb["cwtm"] = arena[:, wcoff:wcoff + 3072].rearrange("p (a b) -> p a b", a=32)
        foff = cv.marks["f"][0]
        def fal(off, *dims):
            n = int(np.prod(dims))
            ap = arena[:, foff + off:foff + off + n]
            if len(dims) == 2:
                ap = ap.rearrange("p (a b) -> p a b", a=dims[0])
            elif len(dims) == 3:
                ap = ap.rearrange("p (a b c) -> p a b c", a=dims[0], b=dims[1])
            return ap, off + n
        sb["qh"], o_ = fal(0, 4096)
        sb["kh"], o_ = fal(o_, 4160)
        sb["vh"], o_ = fal(o_, 4160)
        sb["vtm"], o_ = fal(o_, 32, 65)
        sb["locraw"], o_ = fal(o_, 32, 65)
        sb["compraw"], o_ = fal(o_, 32, 65)
        sb["outtm"], o_ = fal(o_, 32, 64)
        sb["kcw"], o_ = fal(o_, 4096)
        sb["vcw"], o_ = fal(o_, 4096)
        sb["ckx"], o_ = fal(o_, 68)
        sb["cvx"], o_ = fal(o_, 68)
        sb["pta"], o_ = fal(o_, 128)
        sb["ptb"], o_ = fal(o_, 128)
        sb["ptc"], o_ = fal(o_, 128)
        sb["wcombbuf"], o_ = fal(o_, 1536)
        sb["obuf"], o_ = fal(o_, 2, 512)
        assert o_ <= 72 * 512, o_
        sb["qkvtm"], o3 = fal(0, 4096)
        sb["rtm"], o3 = fal(o3, 4096)
        sb["rtmp"], o3 = fal(o3, 2048)
        sb["qcol"], o3 = fal(o3, 16, 512)
        sb["kcol"], o3 = fal(o3, 8, 512)
        assert o3 <= 30208
        sb["kfm"], o3 = fal(0, 4096)
        sb["vtm2"], o3 = fal(o3, 32, 129)
        sb["qfm"], o3 = fal(o3, 4096)
        sb["ohb"], o3 = fal(o3, 4096)
        sb["ptl"], o3 = fal(o3, 2, 512)
        sb["mtile"], o3 = fal(o3, 128)
        n3 = 2 * 32 * 32
        sb["gtall"] = arena[:, foff + o3:foff + o3 + n3].bitcast(F32).rearrange("p (a b) -> p a b", a=32); o3 += n3
        sb["gt01"] = arena[:, foff + o3:foff + o3 + n3].bitcast(F32).rearrange("p (a b) -> p a b", a=32); o3 += n3
        sb["cnt32"] = arena[:, foff + o3:foff + o3 + 64].bitcast(F32); o3 += 64
        sb["means32"] = arena[:, foff + o3:foff + o3 + 64].bitcast(F32); o3 += 64
        sb["meansbf"] = arena[:, foff + o3:foff + o3 + 32]; o3 += 32
        sb["rsum"] = arena[:, foff + o3:foff + o3 + 1024].bitcast(F32); o3 += 1024
        sb["nkrow"] = arena[:, foff + o3:foff + o3 + 256]; o3 += 256
        assert o3 <= 30208, o3
        sb["causal01"] = arena[:, foff + 30208:foff + 30336]
        sb["stair"] = arena[:, foff + 30336:foff + 31872].rearrange("p (a b) -> p a b", a=3)
        sb["fmasktm"] = arena[:, foff + 31872:foff + 33920].bitcast(F32).rearrange("p (a b) -> p a b", a=32)
        sb["ohfm"] = cv.alias_bf("qh__dummy", 1) if False else sb["qh"]
        xoff = cv.marks["xbf"][0]
        sb["hsq"] = arena[:, xoff:xoff + 4096]
        sb["otmp"] = arena[:, xoff + 4096:xoff + 4096 + 4096].bitcast(F32)
        woff = cv.marks["wA"][0]
        sb["cmask"] = arena[:, woff:woff + 4096]
        sb["wtm"] = arena[:, woff:woff + 8192].rearrange("p (a b) -> p a b", a=2)
        sb["costm"] = arena[:, wcoff:wcoff + 2048]
        sb["sintm"] = arena[:, wcoff + 2048:wcoff + 4096]
        ioff = cv.marks["ing_in"][0]
        sb["cw"] = arena[:, ioff:ioff + 8192].bitcast(F32)
        assert cv.off <= ARENA_N, cv.off

        pb = [stack.enter_context(nc.psum_tensor(f"pb{i}", [128, 512], F32)) for i in range(8)]
        pbb = [pb[i][:, :].bitcast(BF16) for i in range(8)]

        C = NS()
        C.nc, C.tn, C.sb, C.pb = nc, tn, sb, pb
        C.pbb = pbb
        C.misc = {}
        p = Prog(nc)
        C.mm = p.sem("mm"); C.sc = p.sem("sc"); C.dv = p.sem("dv"); C.st = p.sem("st")

        clv = p.dma("sync", sb["ident32"][:, :], tn["ident32_c"], "st")
        clv = p.dma("sync", sb["ones_bf"][:, :], tn["ones_bf_c"], "st")
        clv = p.dma("sync", sb["epsc"][:, :], tn["eps_c"], "st")
        if "identbf_c" in tn:
            clv = p.dma("sync", sb["identbf"][:, :], tn["identbf_c"], "st")
        p.wait("scalar", "st", clv)
        p.wait("vector", "st", clv)
        p.wait("tensor", "st", clv)
        p.wait("tensor", "st", clv)
        if "sgate_h" in tn and any(s == "S4" for s in stages):
            clv = p.dma("sync", sb["sgate"][:, :], tn["sgate_h"], "st")
            clv = p.dma("sync", sb["sup"][:, :], tn["sup_h"], "st")
            clv = p.dma("sync", sb["sdown"][:, :], tn["sdown_h"], "st")
            p.wait("scalar", "st", clv)

        cfg_s2 = dict(tag="s2", r_in="R1", r_out="R2", n_ic=72, routed=True, int8=False,
                      gcache="wg_c", ucache="wu_c", dcache="wd_c",
                      preps=[("wg_c", "wg_h", 72, 2048), ("wu_c", "wu_h", 72, 2048), ("wd_c", "wd_h", 16, 9216)])
        cfg_s4 = dict(tag="s4", r_in="R3", r_out="R4", n_ic=64, routed=False, int8=True, prep_mdld=2048,
                      gcache="w1g_c", ucache="w1u_c", dcache="w1d_c",
                      preps=[("w1g_c", "w1g_h", 64, 2048), ("w1u_c", "w1u_h", 64, 2048), ("w1d_c", "w1d_h", 16, 8192)])

        for s in stages:
            if s == "S0":
                emit_ingest(p, C)
            elif s == "S1":
                emit_l0_attn(p, C)
            elif s == "S2":
                emit_ffn(p, C, cfg_s2)
            elif s == "S3":
                emit_l1_attn(p, C)
            elif s == "S4":
                emit_ffn(p, C, cfg_s4)
            elif s == "S5":
                emit_egress(p, C, io_kinds["_egress_src"])
            else:
                raise ValueError(s)

        p.wait("sync", "st", p.val("st"))

        S = p.alloc_sems(stack)
        block = stack.enter_context(nc.Block())
        p.run(block, S)
    return nc, tn



def _consts():
    return {
        "ident32_c": np.eye(128, dtype=np.float32),
        "ones_bf_c": np.ones((128, 128), dtype=BF),
        "eps_c": np.tile(np.array([[1e-5, np.finfo(np.float32).eps, 0.125, 128.0 ** -0.5]], dtype=np.float32), (128, 1)),
        "identbf_c": np.eye(128, dtype=BF),
    }


def _prep_s2_weights(inputs):
    g = np.concatenate([inputs["L0_ffn_sh_gate"].T, inputs["L0_ffn_r_gate"].T], axis=1)
    u = np.concatenate([inputs["L0_ffn_sh_up"].T, inputs["L0_ffn_r_up"].T], axis=1)
    d = np.concatenate([inputs["L0_ffn_sh_down"].T, inputs["L0_ffn_r_down"].T], axis=0)
    wg = np.ascontiguousarray(g.reshape(16, 128, 72, 128).transpose(2, 1, 0, 3).reshape(72, 128, 2048))
    wu = np.ascontiguousarray(u.reshape(16, 128, 72, 128).transpose(2, 1, 0, 3).reshape(72, 128, 2048))
    wd = np.ascontiguousarray(d.reshape(72, 128, 16, 128).transpose(2, 1, 0, 3).reshape(16, 128, 9216))
    rt = np.ascontiguousarray(inputs["L0_ffn_router_weight"].T.reshape(16, 128, 8).transpose(1, 0, 2).reshape(128, 128))
    return {"wg_h": wg, "wu_h": wu, "wd_h": wd, "router_h": rt}


def _prep_s4_weights(inputs):
    g = inputs["L1_ffn_w_gate"]
    u = inputs["L1_ffn_w_up"]
    d = inputs["L1_ffn_w_down"]
    wg = np.ascontiguousarray(g.reshape(16, 128, 64, 128).transpose(2, 1, 0, 3).reshape(64, 128, 2048))
    wu = np.ascontiguousarray(u.reshape(16, 128, 64, 128).transpose(2, 1, 0, 3).reshape(64, 128, 2048))
    wd = np.ascontiguousarray(d.reshape(64, 128, 16, 128).transpose(2, 1, 0, 3).reshape(16, 128, 8192))
    sg = np.ascontiguousarray(inputs["L1_ffn_s_gate"].reshape(64, 128).T)
    su = np.ascontiguousarray(inputs["L1_ffn_s_up"].reshape(64, 128).T)
    sd = np.ascontiguousarray(inputs["L1_ffn_s_down"].reshape(16, 128).T)
    return {"w1g_h": wg, "w1u_h": wu, "w1d_h": wd, "sgate_h": sg, "sup_h": su, "sdown_h": sd}


def _prep_s1_weights(inputs):
    wqkv = inputs["L0_attn_w_qkv"]
    wq = np.ascontiguousarray(wqkv.reshape(16, 128, 48, 128).transpose(2, 1, 0, 3).reshape(48, 128, 2048))
    wcomb = np.ascontiguousarray(inputs["L0_attn_w_comb"].reshape(16, 128, 96).transpose(1, 0, 2).reshape(128, 1536))
    rmsw = np.ascontiguousarray(inputs["L0_attn_rms_w"].reshape(16, 128).T)
    bcomb = np.ascontiguousarray(inputs["L0_attn_b_comb"].reshape(96, 1))
    wout = np.ascontiguousarray(inputs["L0_attn_w_out"].reshape(16, 128, 16, 128).transpose(2, 1, 0, 3).reshape(16, 128, 2048))
    kcw6 = np.ascontiguousarray(inputs["L0_attn_kc_w"].transpose(0, 2, 3, 1).reshape(32, 64, 4096))
    vcw6 = np.ascontiguousarray(inputs["L0_attn_vc_w"].transpose(0, 2, 3, 1).reshape(32, 64, 4096))
    kposT = np.ascontiguousarray(inputs["L0_attn_k_pos"].transpose(0, 2, 1))
    vposT = np.ascontiguousarray(inputs["L0_attn_v_pos"].transpose(0, 2, 1))
    memkT = np.ascontiguousarray(inputs["L0_attn_mem_kv"][0].transpose(0, 2, 1))
    memv = np.ascontiguousarray(inputs["L0_attn_mem_kv"][1])
    ck_seq = np.concatenate([(np.arange(64) + 1) * 64 - 1, np.full(4, -1)])
    cmask = np.where(ck_seq[:, None] < np.arange(T)[None, :], np.float32(NEG), np.float32(0.0)).astype(np.float32)
    kk, qq = np.arange(128)[:, None], np.arange(128)[None, :]
    own = np.where(kk <= qq, 0.0, NEG).astype(np.float32)
    prev = np.where(kk >= qq, 0.0, NEG).astype(np.float32)
    lmask = np.concatenate([own, prev], axis=1)
    return {"wqkv_h": wq, "wcomb_h": wcomb, "rmsw_h": rmsw, "bcomb_h": bcomb, "wout_h": wout,
            "kcw6_h": kcw6, "vcw6_h": vcw6, "kposT_h": kposT, "vposT_h": vposT,
            "kcb_h": np.ascontiguousarray(inputs["L0_attn_kc_b"]), "vcb_h": np.ascontiguousarray(inputs["L0_attn_vc_b"]),
            "memkT_h": memkT, "memv_h": memv, "cmask_h": cmask, "lmask_h": lmask}


def _prep_s3_weights(inputs):
    wq, wk, wv, wo = (inputs["L1_attn_wq"], inputs["L1_attn_wk"], inputs["L1_attn_wv"], inputs["L1_attn_wo"])
    perm = np.concatenate([np.arange(0, 128, 2), np.arange(1, 128, 2)])
    wqp = wq.reshape(2048, 16, 128)[:, :, perm].reshape(2048, 2048)
    wkp = wk.reshape(2048, 8, 128)[:, :, perm].reshape(2048, 1024)
    wall = np.concatenate([wqp, wkp, wv], axis=1)
    wqkvtm = np.ascontiguousarray(wall.reshape(16, 128, 4096))
    wout1 = np.ascontiguousarray(wo.reshape(16, 128, 16, 128).transpose(2, 1, 0, 3).reshape(16, 128, 2048))
    inv_freq = 1.0 / 10000.0 ** (np.arange(0, 128, 2) / 128.0)
    ang = np.arange(T)[:, None] * inv_freq[None, :]
    cosT = np.cos(ang).astype(np.float32).reshape(32, 128, 64).transpose(1, 0, 2).reshape(128, 2048)
    sinT = np.sin(ang).astype(np.float32).reshape(32, 128, 64).transpose(1, 0, 2).reshape(128, 2048)
    kk, qq = np.arange(128)[:, None], np.arange(128)[None, :]
    causal01 = (kk > qq).astype(np.float32)
    stair = np.zeros((128, 3, 512), dtype=np.float32)
    for o in (1, 2, 3):
        for j in range(4):
            if j < o:
                stair[:, o - 1, j * 128:(j + 1) * 128] = NEG
    qbv, lv_ = np.arange(32)[:, None], np.arange(32)[None, :]
    fm = np.where(lv_ > qbv, np.float32(-10000.0), np.float32(0.0))
    fmasktm = np.tile(fm[None, :, :], (128, 1, 1)).astype(np.float32)
    return {"wqkvtm_h": wqkvtm, "wout1_h": wout1, "costm_h": np.ascontiguousarray(cosT),
            "sintm_h": np.ascontiguousarray(sinT), "causal01_h": causal01,
            "stair_h": stair, "fmasktm_h": fmasktm}


FUSED = False

_S1_KEYS = ["wqkv_h", "wcomb_h", "rmsw_h", "bcomb_h", "wout_h", "kcw6_h", "vcw6_h", "kposT_h",
            "vposT_h", "kcb_h", "vcb_h", "memkT_h", "memv_h", "cmask_h", "lmask_h"]
_S1_INT = ["q0", "k0", "v0", "o0", "wqkv_c", "wout_c"]
_S2_KEYS = ["wg_h", "wu_h", "wd_h", "router_h"]
_S2_INT = ["wg_c", "wu_c", "wd_c"]
_S3_KEYS = ["wqkvtm_h", "wout1_h", "costm_h", "sintm_h", "causal01_h", "stair_h", "fmasktm_h"]
_S3_INT = ["q1", "k1", "v1tm", "o1", "wqkvtm_c", "wout1_c"]
_S4_KEYS = ["w1g_h", "w1u_h", "w1d_h", "sgate_h", "sup_h", "sdown_h"]
_S4_INT = ["w1g_c", "w1u_c", "w1d_c"]
_CONSTS = ["ident32_c", "ones_bf_c", "eps_c", "identbf_c"]


def _host_weights(inputs):
    w = {}
    w.update(_consts())
    w.update(_prep_s1_weights(inputs))
    w.update(_prep_s2_weights(inputs))
    w.update(_prep_s3_weights(inputs))
    w.update(_prep_s4_weights(inputs))
    return w


def kernel(**inputs):
    x = inputs["L0_attn_inp"]
    w = _host_weights(inputs)

    if FUSED:
        io = {"x_in": "ExternalInput", "out": "ExternalOutput", "_egress_src": "R4"}
        for rn in ("R0", "R1", "R2", "R3", "R4"):
            io[rn] = "Internal"
        for k_ in _CONSTS + _S1_KEYS + _S2_KEYS + _S3_KEYS + _S4_KEYS:
            io[k_] = "ExternalInput"
        for k_ in _S1_INT + _S2_INT + _S3_INT + _S4_INT:
            io[k_] = "Internal"
        nc, tn = build_program(["S0", "S1", "S2", "S3", "S4", "S5"], io)
        in_maps = []
        for b in range(8):
            m = {"x_in": np.ascontiguousarray(x[b])}
            m.update(w)
            in_maps.append(m)
        res = run_bass_kernel_spmd(nc, in_maps, core_ids=list(range(8)))
        return np.stack([res.results[b]["out"] for b in range(8)], axis=0)

    plan = [
        (["S0"], {"x_in": "ExternalInput", "R0": "ExternalOutput"}, [], "R0"),
        (["S1"], {"R0": "ExternalInput", "R1": "ExternalOutput"}, _S1_KEYS + [("i", k) for k in _S1_INT], "R1"),
        (["S2"], {"R1": "ExternalInput", "R2": "ExternalOutput"}, _S2_KEYS + [("i", k) for k in _S2_INT], "R2"),
        (["S3"], {"R2": "ExternalInput", "R3": "ExternalOutput"}, _S3_KEYS + [("i", k) for k in _S3_INT], "R3"),
        (["S4"], {"R3": "ExternalInput", "R4": "ExternalOutput"}, _S4_KEYS + [("i", k) for k in _S4_INT], "R4"),
        (["S5"], {"R4": "ExternalInput", "out": "ExternalOutput", "_egress_src": "R4"}, [], "out"),
    ]
    cur = [np.ascontiguousarray(x[b]) for b in range(8)]
    cur_key = "x_in"
    for (stages, io, extra, outkey) in plan:
        io = dict(io)
        for k_ in _CONSTS:
            io[k_] = "ExternalInput"
        for e_ in extra:
            if isinstance(e_, tuple):
                io[e_[1]] = "Internal"
            else:
                io[e_] = "ExternalInput"
        nc, tn = build_program(stages, io)
        in_maps = []
        for b in range(8):
            m = {cur_key: cur[b]}
            for k_ in _CONSTS:
                m[k_] = w[k_]
            for e_ in extra:
                if not isinstance(e_, tuple):
                    m[e_] = w[e_]
            in_maps.append(m)
        res = run_bass_kernel_spmd(nc, in_maps, core_ids=list(range(8)))
        cur = [res.results[b][outkey] for b in range(8)]
        cur_key = outkey
    return np.stack(cur, axis=0)



def emit_gemm_to_scratch(p, C, tag, wcache, n_oc, dst_list, ti, xln_ready, evac_fns, prep_dep):
    sb, pb = C.sb, C.pb
    mm, sc = C.mm, C.sc
    ws = p.sem(tag + "w")
    for oc in range(n_oc):
        b = oc % 2
        gp = pb[2 + b]
        wav = p.dma("sync", sb["wA"][:, b, :], wcache[oc], ws,
                    waits=[p.guard(f"{tag}wA{b}"), prep_dep])
        gmv = None
        for dc in range(DC):
            gmv = p.op("tensor", mm, [(ws, wav), xln_ready, p.guard(f"pb{2+b}") if dc == 0 else None],
                       lambda e, S, dc=dc, b=b, gp=gp: e.matmul(gp[:, :], sb["wA"][:, b, dc * 128:(dc + 1) * 128], sb["xln"][:, dc, :],
                                                               start=(dc == 0), stop=(dc == DC - 1)))
        p.set_guard(f"{tag}wA{b}", mm, gmv)
        ev = evac_fns[oc](gp, gmv, b)
        p.set_guard(f"pb{2+b}", ev[0], ev[1])


def emit_l0_attn(p, C):
    sb, pb, tn = C.sb, C.pb, C.tn
    mm, sc, dv, st = C.mm, C.sc, C.dv, C.st
    tag = "s1"
    xs = p.sem(tag + "prep")
    ld = p.sem(tag + "ld")

    p.dma("gpsimd", tn["wqkv_c"][0:24], tn["wqkv_h"][0:24], xs, max_dma_last_dim=8192)
    p.dma("gpsimd", tn["wqkv_c"][24:48], tn["wqkv_h"][24:48], xs, max_dma_last_dim=8192)
    p.dma("gpsimd", sb["wcombbuf"][:, :], tn["wcomb_h"], xs, max_dma_last_dim=2048)
    p.dma("gpsimd", sb["rmsw"][:, :], tn["rmsw_h"], xs, max_dma_last_dim=2048)
    p.dma("gpsimd", sb["bcomb"][0:96, :], tn["bcomb_h"], xs, max_dma_last_dim=2048)
    p.dma("gpsimd", sb["lmask"][:, :], tn["lmask_h"], xs, max_dma_last_dim=2048)
    prep_done = p.val(xs)
    prep_dep = (xs, prep_done)
    p.wait("vector", xs, prep_done)
    p.wait("scalar", xs, prep_done)

    for ti in range(NT):
        ts, te = ti * TT_, (ti + 1) * TT_
        xr = (C.dv, emit_lnprep(p, C, tag, "R0", ti, rms=True))

        def mk_evac(oc, ts=ts, te=te):
            def evac(gp, gmv, b):
                if oc < 48:
                    useq = oc < 16
                    e1 = p.op("scalar", sc, [(mm, gmv), p.guard(f"s1ob{b}")],
                              lambda e, S, gp=gp, b=b, useq=useq: (
                                  e.activation(sb["obuf"][:, b, :], gp[:, :], AF.Copy, scale=sb["epsc"][:, 2:3]) if useq
                                  else e.activation(sb["obuf"][:, b, :], gp[:, :], AF.Copy)))
                    dst = (tn["q0"], oc) if oc < 16 else (tn["k0"], oc - 16) if oc < 32 else (tn["v0"], oc - 32)
                    sv = p.dma("sync", dst[0][dst[1]][:, ts:te], sb["obuf"][:, b, :], st, waits=[(sc, e1)])
                    p.set_guard(f"s1ob{b}", st, sv)
                    return (sc, e1)
                else:
                    e1 = p.op("scalar", sc, [(mm, gmv)],
                              lambda e, S, gp=gp, ts=ts, te=te: e.activation(sb["cw"][0:96, ts:te], gp[0:96, :], AF.Sigmoid, bias=sb["bcomb"][0:96, 0:1]))
                    return (sc, e1)
            return evac

        emit_gemm_to_scratch(p, C, tag, tn["wqkv_c"], 48, None, ti, xr,
                             [mk_evac(oc) for oc in range(48)], prep_dep)
        cmv = None
        for dc in range(DC):
            cmv = p.op("tensor", mm, [xr, prep_dep, p.guard("pb4") if dc == 0 else None],
                       lambda e, S, dc=dc: e.matmul(pb[4][0:96, :], sb["wcombbuf"][:, dc * 96:(dc + 1) * 96], sb["xln"][:, dc, :],
                                                    start=(dc == 0), stop=(dc == DC - 1)))
        e1 = mk_evac(48)(pb[4], cmv, 0)
        p.set_guard("pb4", e1[0], e1[1])
        p.set_guard("xln", mm, p.val(mm))
        p.set_guard("xbf", sc, p.val(sc))

    if "dbg_cw2" in tn:
        p.dma("sync", tn["dbg_cw2"], sb["cw"][0:96, :], st, waits=[(C.sc, p.val(sc))])
    qkv_done_st = p.val(st)
    qkv_done_mm = p.val(mm)
    cmv = p.dma("gpsimd", sb["cmask"][0:68, :], tn["cmask_h"], xs,
                waits=[(mm, qkv_done_mm)], max_dma_last_dim=2048)
    cmask_dep = (xs, cmv)

    for n in range(32):
        tv = p.op("tensor", mm, [(C.sc, p.val(sc)), p.guard(f"pb{5 + n % 2}")],
                  lambda e, S, n=n: e.transpose(pb[5 + n % 2][0:128, 0:96], sb["cw"][0:96, n * 128:(n + 1) * 128], sb["ident32"][0:96, 0:96]))
        ev = p.op("vector", dv, [(mm, tv)],
                  lambda e, S, n=n: e.tensor_copy(sb["cwtm"][:, n, :], pb[5 + n % 2][0:128, 0:96]))
        p.set_guard(f"pb{5 + n % 2}", dv, ev)

    for h in range(32):
        c0, p0 = h // 2, (h % 2) * 64
        ldg = p.sem(tag + "ldg")
        lq = p.dma("sync", sb["qh"][0:64, :], tn["q0"][c0][p0:p0 + 64, :], ld, waits=[(st, qkv_done_st), p.guard("qh")])
        lk = p.dma("sync", sb["kh"][0:64, 0:4096], tn["k0"][c0][p0:p0 + 64, :], ld, waits=[p.guard("kh")])
        lk2 = p.dma("gpsimd", sb["kh"][0:64, 4096:4160], tn["kposT_h"][h], ldg, waits=[p.guard("kh")])
        lv = p.dma("sync", sb["vh"][0:64, 0:4096], tn["v0"][c0][p0:p0 + 64, :], ld, waits=[p.guard("vh")])
        lv2 = p.dma("gpsimd", sb["vh"][0:64, 4096:4160], tn["vposT_h"][h], ldg, waits=[p.guard("vh")])
        lkc = p.dma("gpsimd", sb["kcw"][0:64, :], tn["kcw6_h"][h], ldg, waits=[p.guard("kcw")], max_dma_last_dim=8192)
        lvc = p.dma("gpsimd", sb["vcw"][0:64, :], tn["vcw6_h"][h], ldg, waits=[p.guard("vcw")], max_dma_last_dim=8192)
        lkb = p.dma("gpsimd", sb["kcb"][0:64, 0:1], tn["kcb_h"][h][:, None], ldg, waits=[p.guard("kcb")])
        lvb = p.dma("gpsimd", sb["vcbrow"][64:65, 0:64], tn["vcb_h"][h][None, :], ldg, waits=[p.guard("vcbrow")])
        lmk = p.dma("gpsimd", sb["ckx"][0:64, 64:68], tn["memkT_h"][h], ldg, waits=[p.guard("ckx")])
        lmv = p.dma("gpsimd", sb["cvx"][64:68, 0:64], tn["memv_h"][h], ldg, waits=[p.guard("cvx")])
        ldv_all = p.val(ld)
        ldg_all = p.val(ldg)

        ms1 = p.op("vector", dv, [p.guard("vtm")],
                   lambda e, S: e.memset(sb["vtm"][:, :, 64:65], 1.0))
        ms2 = p.op("vector", dv, [],
                   lambda e, S: e.memset(sb["cvx"][0:68, 64:65], 1.0))
        vtm_ev = []
        for n in range(32):
            tv = p.op("tensor", mm, [(ld, ldv_all), (ldg, ldg_all), p.guard(f"pb{5 + n % 2}")],
                      lambda e, S, n=n: e.transpose(C.pbb[5 + n % 2][0:128, 0:64], sb["vh"][0:64, n * 128:(n + 1) * 128], sb["identbf"][0:64, 0:64]))
            ev = p.op("vector", dv, [(mm, tv)],
                      lambda e, S, n=n: e.tensor_copy(sb["vtm"][:, n, 0:64], C.pbb[5 + n % 2][0:128, 0:64]))
            p.set_guard(f"pb{5 + n % 2}", dv, ev)
            vtm_ev.append(ev)

        khv = sb["kh"].rearrange("p (n t) -> p t n", t=64)
        vhv = sb["vh"].rearrange("p (n t) -> p t n", t=64)
        ckmv = None
        for t in range(64):
            ckmv = p.op("tensor", mm, [(ld, ldv_all), (ldg, ldg_all), p.guard("pb0") if t == 0 else None],
                        lambda e, S, t=t: e.matmul(pb[0][0:64, 0:65], sb["kcw"].rearrange("p (t o) -> p t o", t=64)[0:64, t, :],
                                                   khv[0:64, t, :], start=(t == 0), stop=(t == 63)))
        bcv = p.op("vector", dv, [(mm, ckmv), (ldg, ldg_all)],
                   lambda e, S: e.tensor_tensor(sb["kcb"][0:64, 1:2], pb[0][0:64, 64:65], sb["kcb"][0:64, 0:1], op=OP.add), selfwait=True)
        ckev = p.op("vector", dv, [(dv, bcv)],
                    lambda e, S: e.tensor_scalar(sb["ckx"][0:64, 0:64], pb[0][0:64, 0:64], sb["kcb"][0:64, 1:2], None, op0=OP.add))
        p.set_guard("pb0", dv, ckev)

        cvmv = None
        for t in range(64):
            cvmv = p.op("tensor", mm, [(ld, ldv_all), (ldg, ldg_all), p.guard("pb1") if t == 0 else None],
                        lambda e, S, t=t: e.matmul(pb[1][0:65, 0:64], vhv[0:64, t, :],
                                                   sb["vcw"].rearrange("p (t o) -> p t o", t=64)[0:64, t, :],
                                                   start=(t == 0), stop=(t == 63)))
        bvv = p.op("vector", dv, [(mm, cvmv), (ldg, ldg_all)],
                   lambda e, S: e.tensor_tensor(sb["vcbrow"][64:65, 64:128], pb[1][64:65, 0:64], sb["vcbrow"][64:65, 0:64], op=OP.add), selfwait=True)
        bfv = p.op("vector", dv, [],
                   lambda e, S: e.tensor_copy(sb["vcbrow_bf"][64:65, 0:64], sb["vcbrow"][64:65, 64:128]), selfwait=True)
        cvk = p.op("tensor", mm, [(dv, bfv)],
                   lambda e, S: e.matmul(pb[1][0:64, 0:64], sb["ones_bf"][64:65, 0:64], sb["vcbrow_bf"][64:65, 0:64],
                                         start=False, stop=True))
        cvev = p.op("scalar", sc, [(mm, cvk)],
                    lambda e, S: e.activation(sb["cvx"][0:64, 0:64], pb[1][0:64, 0:64], AF.Copy))
        p.set_guard("pb1", sc, cvev)

        sqv = p.op("scalar", sc, [(ld, ldv_all), (ldg, ldg_all)],
                   lambda e, S: e.activation(sb["hsq"][0:64, :], sb["kh"][0:64, 0:4096], AF.Square))
        for tt in range(8):
            b7 = pb[7]
            mv = p.op("tensor", mm, [(sc, sqv), p.guard("pb7")],
                      lambda e, S, tt=tt: e.matmul(b7[:, :], sb["ones_bf"][0:64, :], sb["hsq"][0:64, tt * 512:(tt + 1) * 512],
                                                   start=True, stop=True))
            rv = p.op("vector", dv, [(mm, mv)],
                      lambda e, S, tt=tt: e.reduce_max(sb["nrm"][:, tt:tt + 1], b7[:, :], axis=AX.X))
            p.set_guard("pb7", dv, rv)
        kmx = p.op("vector", dv, [], lambda e, S: e.reduce_max(sb["nrm"][:, 8:9], sb["nrm"][:, 0:8], axis=AX.X), selfwait=True)
        sqv = p.op("scalar", sc, [(ld, ldv_all), (dv, kmx)],
                   lambda e, S: e.activation(sb["hsq"][0:64, :], sb["qh"][0:64, 0:4096], AF.Square))
        for tt in range(8):
            b7 = pb[7]
            mv = p.op("tensor", mm, [(sc, sqv), p.guard("pb7")],
                      lambda e, S, tt=tt: e.matmul(b7[:, :], sb["ones_bf"][0:64, :], sb["hsq"][0:64, tt * 512:(tt + 1) * 512],
                                                   start=True, stop=True))
            rv = p.op("vector", dv, [(mm, mv)],
                      lambda e, S, tt=tt: e.reduce_max(sb["nrm"][:, tt:tt + 1], b7[:, :], axis=AX.X))
            p.set_guard("pb7", dv, rv)
        p.op("vector", dv, [], lambda e, S: e.reduce_max(sb["nrm"][:, 9:10], sb["nrm"][:, 0:8], axis=AX.X), selfwait=True)
        p.op("vector", dv, [], lambda e, S: e.tensor_tensor(sb["nrm"][:, 10:11], sb["nrm"][:, 8:9], sb["nrm"][:, 9:10], op=OP.mult), selfwait=True)
        qkv_ = p.op("scalar", sc, [(dv, p.val(dv))], lambda e, S: e.activation(sb["nrm"][:, 11:12], sb["nrm"][:, 10:11], AF.Sqrt))
        negm = p.op("vector", dv, [(sc, qkv_)], lambda e, S: e.tensor_scalar(sb["nrm"][:, 12:13], sb["nrm"][:, 11:12], -1.0, None, op0=OP.mult), selfwait=True)
        sqv2 = p.op("scalar", sc, [(dv, ckev), (dv, negm)],
                    lambda e, S: e.activation(sb["hsq"][0:64, 0:68], sb["ckx"][0:64, 0:68], AF.Square))
        mv = p.op("tensor", mm, [(sc, sqv2), p.guard("pb7")],
                  lambda e, S: e.matmul(pb[7][:, 0:68], sb["ones_bf"][0:64, :], sb["hsq"][0:64, 0:68], start=True, stop=True))
        rv = p.op("vector", dv, [(mm, mv)], lambda e, S: e.reduce_max(sb["nrm"][:, 13:14], pb[7][:, 0:68], axis=AX.X))
        p.set_guard("pb7", dv, rv)
        p.op("vector", dv, [], lambda e, S: e.tensor_tensor(sb["nrm"][:, 14:15], sb["nrm"][:, 9:10], sb["nrm"][:, 13:14], op=OP.mult), selfwait=True)
        csq = p.op("scalar", sc, [(dv, p.val(dv))], lambda e, S: e.activation(sb["nrm"][:, 15:16], sb["nrm"][:, 14:15], AF.Sqrt))
        negmc = p.op("vector", dv, [(sc, csq)], lambda e, S: e.tensor_scalar(sb["nrm"][:, 16:17], sb["nrm"][:, 15:16], -1.0, None, op0=OP.mult), selfwait=True)
        shift_dep = (dv, negmc)

        for qb in range(32):
            qsl = sb["qh"][0:64, qb * 128:(qb + 1) * 128]
            if qb > 0:
                m2 = p.op("tensor", mm, [(ld, ldv_all), (ldg, ldg_all), p.guard("pb2")],
                          lambda e, S, qb=qb, qsl=qsl: e.matmul(pb[2][:, 0:128], sb["kh"][0:64, (qb - 1) * 128:qb * 128], qsl, start=True, stop=False))
            m3 = p.op("tensor", mm, [(ld, ldv_all), (ldg, ldg_all), p.guard("pb3")],
                      lambda e, S, qb=qb, qsl=qsl: e.matmul(pb[3][:, 0:128], sb["kh"][0:64, qb * 128:(qb + 1) * 128], qsl, start=True, stop=False))
            m4 = p.op("tensor", mm, [(dv, ckev), p.guard("pb4")],
                      lambda e, S, qb=qb, qsl=qsl: e.matmul(pb[4][0:68, 0:128], sb["ckx"][0:64, 0:68], qsl, start=True, stop=False))
            if qb > 0:
                m2 = p.op("tensor", mm, [prep_dep],
                          lambda e, S: e.matmul(pb[2][:, 0:128], sb["identbf"][:, :], sb["lmask"][:, 128:256], start=False, stop=True))
            m3 = p.op("tensor", mm, [prep_dep],
                      lambda e, S: e.matmul(pb[3][:, 0:128], sb["identbf"][:, :], sb["lmask"][:, 0:128], start=False, stop=True))
            m4 = p.op("tensor", mm, [cmask_dep],
                      lambda e, S, qb=qb: e.matmul(pb[4][0:68, 0:128], sb["identbf"][0:68, 0:68], sb["cmask"][0:68, qb * 128:(qb + 1) * 128], start=False, stop=True))
            if qb > 0:
                e2 = p.op("scalar", sc, [(mm, m2), shift_dep, p.guard("pta")],
                          lambda e, S: e.activation(sb["pta"][:, 0:128], pb[2][:, 0:128], AF.Exp, bias=sb["nrm"][:, 12:13]))
                p.set_guard("pb2", sc, e2)
            e3 = p.op("scalar", sc, [(mm, m3), shift_dep, p.guard("ptb")],
                      lambda e, S: e.activation(sb["ptb"][:, 0:128], pb[3][:, 0:128], AF.Exp, bias=sb["nrm"][:, 12:13]))
            p.set_guard("pb3", sc, e3)
            if h == 1 and qb == 5 and "dbg_ckx" in tn:
                dbv = p.op("scalar", sc, [(mm, m4)],
                           lambda e, S: e.activation(sb["hsq"].bitcast(F32)[0:68, 0:128], pb[4][0:68, 0:128], AF.Copy))
                p.dma("sync", tn["dbg_csim"], sb["hsq"].bitcast(F32)[0:68, 0:128], st, waits=[(sc, dbv)])
            e4 = p.op("scalar", sc, [(mm, m4), shift_dep, p.guard("ptc")],
                      lambda e, S: e.activation(sb["ptc"][0:68, 0:128], pb[4][0:68, 0:128], AF.Exp, bias=sb["nrm"][0:68, 16:17]))
            p.set_guard("pb4", sc, e4)
            if h == 1 and qb == 5 and "dbg_ckx" in tn:
                p.dma("sync", tn["dbg_ptc"], sb["ptc"][0:68, 0:128], st, waits=[(sc, e4)])
            pvv = None
            if qb > 0:
                pvv = p.op("tensor", mm, [(sc, e2), (dv, ms1), (dv, vtm_ev[qb - 1]), p.guard("pb5")],
                           lambda e, S, qb=qb: e.matmul(pb[5][:, 0:65], sb["pta"][:, 0:128], sb["vtm"][:, qb - 1, :], start=True, stop=False))
            pvv = p.op("tensor", mm, [(sc, e3), (dv, ms1), (dv, vtm_ev[qb])] + ([p.guard("pb5")] if qb == 0 else []),
                       lambda e, S, qb=qb: e.matmul(pb[5][:, 0:65], sb["ptb"][:, 0:128], sb["vtm"][:, qb, :], start=(qb == 0), stop=True))
            pta_free = pvv
            cvv = p.op("tensor", mm, [(sc, e4), (sc, cvev), (dv, ms2), (ldg, ldg_all), p.guard("pb6")],
                       lambda e, S: e.matmul(pb[6][:, 0:65], sb["ptc"][0:68, 0:128], sb["cvx"][0:68, 0:65], start=True, stop=True))
            p.set_guard("pta", mm, pvv)
            p.set_guard("ptb", mm, pvv)
            p.set_guard("ptc", mm, cvv)
            le = p.op("scalar", sc, [(mm, pvv), p.guard("locraw") if qb == 0 else None],
                      lambda e, S, qb=qb: e.activation(sb["locraw"][:, qb, :], pb[5][:, 0:65], AF.Copy))
            p.set_guard("pb5", sc, le)
            ce = p.op("scalar", sc, [(mm, cvv)],
                      lambda e, S, qb=qb: e.activation(sb["compraw"][:, qb, :], pb[6][:, 0:65], AF.Copy))
            p.set_guard("pb6", sc, ce)
        raws_done = p.val(sc)

        wlv = p.op("vector", dv, [(sc, raws_done)],
                   lambda e, S, h=h: e.tensor_tensor(sb["nb"][:, 0:32], sb["cwtm"][:, :, 3 * h:3 * h + 1].rearrange("p a b -> p (a b)"),
                                                     sb["cwtm"][:, :, 3 * h + 1:3 * h + 2].rearrange("p a b -> p (a b)"), op=OP.add), selfwait=True)
        p.op("vector", dv, [],
             lambda e, S: e.reciprocal(sb["nb"][:, 32:64], sb["locraw"][:, :, 64:65].rearrange("p a b -> p (a b)")), selfwait=True)
        p.op("vector", dv, [],
             lambda e, S: e.tensor_tensor(sb["nb"][:, 0:32], sb["nb"][:, 0:32], sb["nb"][:, 32:64], op=OP.mult), selfwait=True)
        p.op("vector", dv, [],
             lambda e, S: e.reciprocal(sb["nb"][:, 64:96], sb["compraw"][:, :, 64:65].rearrange("p a b -> p (a b)")), selfwait=True)
        wcv = p.op("vector", dv, [],
                   lambda e, S, h=h: e.tensor_tensor(sb["nb"][:, 64:96], sb["cwtm"][:, :, 3 * h + 2:3 * h + 3].rearrange("p a b -> p (a b)"),
                                                     sb["nb"][:, 64:96], op=OP.mult), selfwait=True)
        _l0, _w0 = _bcast(sb["locraw"][:, :, 0:64], sb["nb"][:, 0:32].rearrange("p (a b) -> p a b", b=1))
        _c0, _w1 = _bcast(sb["compraw"][:, :, 0:64], sb["nb"][:, 64:96].rearrange("p (a b) -> p a b", b=1))
        t1v = p.op("vector", dv, [],
                   lambda e, S: e.tensor_tensor(sb["otmp"].rearrange("p (a b) -> p a b", a=32), _l0, _w0, op=OP.mult), selfwait=True)
        t2v = p.op("vector", dv, [],
                   lambda e, S: e.tensor_tensor(sb["outtm"], _c0, _w1, op=OP.mult), selfwait=True)
        t3v = p.op("vector", dv, [],
                   lambda e, S: e.tensor_tensor(sb["outtm"], sb["otmp"].rearrange("p (a b) -> p a b", a=32), sb["outtm"], op=OP.add), selfwait=True)
        p.set_guard("locraw", dv, t3v)

        for n in range(32):
            tv = p.op("tensor", mm, [(dv, t3v), p.guard(f"pb{5 + n % 2}")],
                      lambda e, S, n=n: e.transpose(C.pbb[5 + n % 2][0:64, 0:128], sb["outtm"][:, n, :], sb["identbf"][:, :]))
            ev = p.op("vector", dv, [(mm, tv)],
                      lambda e, S, n=n: e.tensor_copy(sb["ohfm"][0:64, n * 128:(n + 1) * 128], C.pbb[5 + n % 2][0:64, 0:128]))
            p.set_guard(f"pb{5 + n % 2}", dv, ev)
        ohv = p.val(dv)
        sv = p.dma("sync", tn["o0"][c0][p0:p0 + 64, :], sb["ohfm"][0:64, :], st, waits=[(dv, ohv)])
        if h == 1 and "dbg_ckx" in tn:
            p.dma("sync", tn["dbg_cwtm"], sb["cwtm"][:, :, :], st, waits=[(dv, t3v)])
            p.dma("sync", tn["dbg_cw"], sb["cw"][0:96, :], st, waits=[(dv, t3v)])
            p.dma("sync", tn["dbg_nb"], sb["nb"][:, :], st, waits=[(dv, t3v)])
            p.dma("sync", tn["dbg_ckx"], sb["ckx"][0:68, :], st, waits=[(dv, ckev), (mm, p.val(mm))])
            p.dma("sync", tn["dbg_cvx"], sb["cvx"][0:68, :], st, waits=[(sc, cvev)])
            p.dma("sync", tn["dbg_loc"], sb["locraw"][:, :, :], st, waits=[(sc, raws_done)])
            p.dma("sync", tn["dbg_cmp"], sb["compraw"][:, :, :], st, waits=[(sc, raws_done)])
            p.dma("sync", tn["dbg_nrm"], sb["nrm"][:, :], st, waits=[(dv, negmc)])
            p.dma("sync", tn["dbg_otm"], sb["outtm"][:, :, :], st, waits=[(dv, t3v)])
        p.set_guard("qh", st, sv)
        p.set_guard("kh", mm, p.val(mm))
        p.set_guard("vh", mm, p.val(mm))
        p.set_guard("vtm", mm, p.val(mm))
        p.set_guard("kcw", mm, ckmv)
        p.set_guard("vcw", mm, cvmv)
        p.set_guard("ckx", mm, p.val(mm))
        p.set_guard("cvx", mm, p.val(mm))
        p.set_guard("kcb", dv, ckev)
        p.set_guard("vcbrow", mm, cvk)

    emit_matout(p, C, "s1o", tn["o0"], tn["wout_c"], ("wout_c", "wout_h", 16, 2048), "R0", "R1")


def emit_matout(p, C, tag, src_scratch, wcache, prep_spec, r_in, r_out):
    sb, pb, tn = C.sb, C.pb, C.tn
    mm, sc, dv, st = C.mm, C.sc, C.dv, C.st
    ws = p.sem(tag + "w")
    xs = p.sem(tag + "prep")
    cache_name, host_name, n_oc, row = prep_spec
    p.dma("gpsimd", tn[cache_name], tn[host_name], xs, max_dma_last_dim=8192)
    prep_dep = (xs, p.val(xs))
    ld = p.sem(tag + "ld")
    for ti in range(NT):
        ts, te = ti * TT_, (ti + 1) * TT_
        lo = p.dma("sync", sb["xln"][:, :, :],
                   src_scratch.rearrange("c p t -> p c t")[:, :, ts:te], ld,
                   waits=[p.guard("xln")])
        y_done = None
        for hc in range(n_oc):
            b = hc % 2
            gp = pb[2 + b]
            wav = p.dma("sync", sb["wA"][:, b, :], wcache[hc], ws,
                        waits=[p.guard(f"{tag}wA{b}"), prep_dep])
            gmv = None
            for dc in range(DC):
                gmv = p.op("tensor", mm, [(ws, wav), (ld, lo), p.guard(f"pb{2+b}") if dc == 0 else None],
                           lambda e, S, dc=dc, b=b, gp=gp: e.matmul(gp[:, :], sb["wA"][:, b, dc * 128:(dc + 1) * 128], sb["xln"][:, dc, :],
                                                                   start=(dc == 0), stop=(dc == DC - 1)))
            p.set_guard(f"{tag}wA{b}", mm, gmv)
            yv = p.op("scalar", sc, [(mm, gmv), p.guard("ybf") if hc == 0 else None],
                      lambda e, S, hc=hc, gp=gp: e.activation(sb["ybf"][:, hc, :], gp[:, :], AF.Copy))
            p.set_guard(f"pb{2+b}", sc, yv)
            y_done = yv
        res_done = emit_postln_residual(p, C, tag, r_in, r_out, ti, (sc, y_done))
        p.set_guard("xbf", C.dv, res_done)
        p.set_guard("xln", mm, p.val(mm))



def emit_l1_attn(p, C):
    sb, pb, tn = C.sb, C.pb, C.tn
    mm, sc, dv, st = C.mm, C.sc, C.dv, C.st
    tag = "s3"
    xs = p.sem(tag + "prep")
    ws = p.sem(tag + "w")
    ld = p.sem(tag + "ld")

    p.dma("gpsimd", tn["wqkvtm_c"][0:8], tn["wqkvtm_h"][0:8], xs, max_dma_last_dim=8192)
    p.dma("gpsimd", tn["wqkvtm_c"][8:16], tn["wqkvtm_h"][8:16], xs, max_dma_last_dim=8192)
    p.dma("gpsimd", sb["costm"][:, :], tn["costm_h"], xs, max_dma_last_dim=2048)
    p.dma("gpsimd", sb["sintm"][:, :], tn["sintm_h"], xs, max_dma_last_dim=2048)
    p.dma("gpsimd", sb["causal01"][:, :], tn["causal01_h"], xs, max_dma_last_dim=2048)
    p.dma("gpsimd", sb["stair"][:, :, :], tn["stair_h"], xs, max_dma_last_dim=2048)
    prep_done = p.val(xs)
    prep_dep = (xs, prep_done)
    fmv = p.dma("sync", sb["fmasktm"][:, :, :], tn["fmasktm_h"], st)
    p.wait("vector", st, fmv)

    for ti in range(NT):
        ts, te = ti * TT_, (ti + 1) * TT_
        xr = (C.dv, emit_lnprep(p, C, tag, "R2", ti, rms=False))
        for tk in range(4):
            tidx = ti * 4 + tk
            mmv = None
            for dc in range(DC):
                wv = p.dma("sync", sb["wtm"][:, dc % 2, :], tn["wqkvtm_c"][dc], ws,
                           waits=[p.guard(f"wtm{dc % 2}"), prep_dep])
                for sl in range(8):
                    mmv = p.op("tensor", mm, [(ws, wv), xr, p.guard(f"pb{sl}") if dc == 0 else None],
                               lambda e, S, dc=dc, sl=sl, tk=tk: e.matmul(
                                   pb[sl][:, :], sb["xln"][:, dc, tk * 128:(tk + 1) * 128],
                                   sb["wtm"][:, dc % 2, sl * 512:(sl + 1) * 512],
                                   start=(dc == 0), stop=(dc == DC - 1)))
                p.set_guard(f"wtm{dc % 2}", mm, mmv)
            evv = None
            for sl in range(8):
                evv = p.op("scalar", sc, [(mm, mmv), p.guard("qkvtm") if sl == 0 else None],
                           lambda e, S, sl=sl: (
                               e.activation(sb["qkvtm"][:, sl * 512:(sl + 1) * 512], pb[sl][:, :], AF.Copy, scale=sb["epsc"][:, 3:4]) if sl < 4
                               else e.activation(sb["qkvtm"][:, sl * 512:(sl + 1) * 512], pb[sl][:, :], AF.Copy)))
            coff = tidx * 64
            ropev = None
            for (nh, off) in ((16, 0), (8, 2048)):
                ein = sb["qkvtm"][:, off:off + nh * 128].rearrange("p (h x) -> p h x", h=nh)
                eout = sb["rtm"][:, off:off + nh * 128].rearrange("p (h x) -> p h x", h=nh)
                t1 = sb["rtmp"][:, 0:nh * 64].rearrange("p (h x) -> p h x", h=nh)
                t2 = sb["rtmp"][:, 1024:1024 + nh * 64].rearrange("p (h x) -> p h x", h=nh)
                cse = sb["costm"][:, coff:coff + 64].rearrange("p (a b) -> p a b", a=1)
                sne = sb["sintm"][:, coff:coff + 64].rearrange("p (a b) -> p a b", a=1)
                _, csb = _bcast(ein[:, :, 0:64], cse)
                _, snb = _bcast(ein[:, :, 0:64], sne)
                p.op("vector", dv, [(sc, evv), prep_dep, p.guard("rtm") if off == 0 else None],
                     lambda e, S, ein=ein, csb=csb, t1=t1: e.tensor_tensor(t1, ein[:, :, 0:64], csb, op=OP.mult))
                p.op("vector", dv, [],
                     lambda e, S, ein=ein, snb=snb, t2=t2: e.tensor_tensor(t2, ein[:, :, 64:128], snb, op=OP.mult))
                p.op("vector", dv, [],
                     lambda e, S, eout=eout, t1=t1, t2=t2: e.tensor_tensor(eout[:, :, 0:64], t1, t2, op=OP.subtract))
                p.op("vector", dv, [],
                     lambda e, S, ein=ein, snb=snb, t1=t1: e.tensor_tensor(t1, ein[:, :, 0:64], snb, op=OP.mult))
                p.op("vector", dv, [],
                     lambda e, S, ein=ein, csb=csb, t2=t2: e.tensor_tensor(t2, ein[:, :, 64:128], csb, op=OP.mult))
                ropev = p.op("vector", dv, [],
                             lambda e, S, eout=eout, t1=t1, t2=t2: e.tensor_tensor(eout[:, :, 64:128], t1, t2, op=OP.add))
            trevs = []
            for c in range(24):
                b = c % 2
                tv = p.op("tensor", mm, [(dv, ropev), p.guard(f"pb{b}")],
                          lambda e, S, c=c, b=b: e.transpose(C.pbb[b][0:128, 0:128], sb["rtm"][:, c * 128:(c + 1) * 128], sb["identbf"][:, :]))
                dst = sb["qcol"] if c < 16 else sb["kcol"]
                dc_ = c if c < 16 else c - 16
                ev = p.op("vector", dv, [(mm, tv), p.guard("qcol")],
                          lambda e, S, dst=dst, dc_=dc_, tk=tk, b=b: e.tensor_copy(dst[:, dc_, tk * 128:(tk + 1) * 128], C.pbb[b][0:128, 0:128]))
                p.set_guard(f"pb{b}", dv, ev)
                trevs.append(ev)
            p.set_guard("qkvtm", dv, ropev)
            p.set_guard("rtm", mm, p.val(mm))
            vv_ = p.dma("sync", tn["v1tm"][tidx], sb["qkvtm"][:, 3072:4096], st, waits=[(sc, evv)])
        qv = p.dma("sync", tn["q1"].rearrange("c p t -> p c t")[:, :, ts:te], sb["qcol"][:, :, :], st,
                   waits=[(dv, p.val(dv))])
        kv2 = p.dma("sync", tn["k1"].rearrange("c p t -> p c t")[:, :, ts:te], sb["kcol"][:, :, :], st,
                    waits=[(dv, p.val(dv))])
        p.set_guard("qcol", st, kv2)
        p.set_guard("xln", mm, p.val(mm))
        p.set_guard("xbf", sc, p.val(sc))
    qkv_done_st = p.val(st)

    for kv in range(8):
        lk = p.dma("sync", sb["kfm"][:, :], tn["k1"][kv], ld, waits=[(st, qkv_done_st), p.guard("kfm")])
        lv = p.dma("sync", sb["vtm2"][:, :, 0:128],
                   tn["v1tm"].rearrange("c p d -> p c d")[:, :, kv * 128:(kv + 1) * 128], ld, waits=[p.guard("vtm2")])
        ldv = p.val(ld)
        msv = p.op("vector", dv, [p.guard("vtm2ones")],
                   lambda e, S: e.memset(sb["vtm2"][:, :, 128:129], 1.0))
        bmv = p.op("vector", dv, [(ld, ldv)],
                   lambda e, S: e.tensor_reduce(sb["means32"][:, :], sb["kfm"].rearrange("p (n x) -> p n x", x=128), axis=AX.X, op=OP.add))
        bmv = p.op("vector", dv, [],
                   lambda e, S: e.tensor_scalar(sb["meansbf"][:, :], sb["means32"][:, :], 1.0 / 128.0, None, op0=OP.mult), selfwait=True)
        sqv = p.op("scalar", sc, [(ld, ldv)],
                   lambda e, S: e.activation(sb["hsq"][:, 0:4096], sb["kfm"][:, :], AF.Square))
        for tt in range(8):
            mv = p.op("tensor", mm, [(sc, sqv), p.guard("pb7")],
                      lambda e, S, tt=tt: e.matmul(pb[7][:, :], sb["ones_bf"][:, :], sb["hsq"][:, tt * 512:(tt + 1) * 512], start=True, stop=True))
            rv = p.op("vector", dv, [(mm, mv)],
                      lambda e, S, tt=tt: e.reduce_max(sb["nrm"][:, tt:tt + 1], pb[7][:, :], axis=AX.X))
            p.set_guard("pb7", dv, rv)
        kmx = p.op("vector", dv, [], lambda e, S: e.reduce_max(sb["nrm"][:, 8:9], sb["nrm"][:, 0:8], axis=AX.X), selfwait=True)

        for qi in range(2):
            qh2 = 2 * kv + qi
            lq = p.dma("sync", sb["qfm"][:, :], tn["q1"][qh2], ld, waits=[p.guard("qfm")])
            lqv = p.val(ld)
            sqv = p.op("scalar", sc, [(ld, lqv), (dv, kmx)],
                       lambda e, S: e.activation(sb["hsq"][:, 0:4096], sb["qfm"][:, :], AF.Square))
            for tt in range(8):
                mv = p.op("tensor", mm, [(sc, sqv), p.guard("pb7")],
                          lambda e, S, tt=tt: e.matmul(pb[7][:, :], sb["ones_bf"][:, :], sb["hsq"][:, tt * 512:(tt + 1) * 512], start=True, stop=True))
                rv = p.op("vector", dv, [(mm, mv)],
                          lambda e, S, tt=tt: e.reduce_max(sb["nrm"][:, 9 + tt:10 + tt], pb[7][:, :], axis=AX.X))
                p.set_guard("pb7", dv, rv)
            p.op("vector", dv, [], lambda e, S: e.reduce_max(sb["nrm"][:, 0:1], sb["nrm"][:, 9:17], axis=AX.X), selfwait=True)
            p.op("vector", dv, [], lambda e, S: e.tensor_tensor(sb["nrm"][:, 10:11], sb["nrm"][:, 8:9], sb["nrm"][:, 0:1], op=OP.mult), selfwait=True)
            sq2 = p.op("scalar", sc, [(dv, p.val(dv))], lambda e, S: e.activation(sb["nrm"][:, 11:12], sb["nrm"][:, 10:11], AF.Sqrt))
            negm = p.op("vector", dv, [(sc, sq2)],
                        lambda e, S: e.tensor_scalar(sb["nrm"][:, 12:13], sb["nrm"][:, 11:12], -1.0, None, op0=OP.mult), selfwait=True)
            shift_dep = (dv, negm)

            gevs = []
            for qb in range(32):
                gv = p.op("tensor", mm, [(dv, bmv), (ld, lqv), p.guard("pb6")],
                          lambda e, S, qb=qb: e.matmul(pb[6][:, 0:32], sb["qfm"][:, qb * 128:(qb + 1) * 128], sb["meansbf"][:, :], start=True, stop=True))
                ev = p.op("vector", dv, [(mm, gv), (st, fmv)],
                          lambda e, S, qb=qb: e.tensor_tensor(sb["gtall"][:, qb, :], pb[6][:, 0:32], sb["fmasktm"][:, qb, :], op=OP.add))
                p.set_guard("pb6", dv, ev)
                gevs.append(ev)
            own = sb["gtall"].rearrange("p a b -> p (a b)")[:, 0::33].rearrange("p (a b) -> p a b", b=1)
            g3, ownb = _bcast(sb["gtall"][:, :, :], own)
            p.op("vector", dv, [(dv, gevs[-1])],
                 lambda e, S, g3=g3, ownb=ownb: e.tensor_tensor(sb["gt01"][:, :, :], g3, ownb, op=OP.is_gt), selfwait=True)
            p.op("vector", dv, [],
                 lambda e, S: e.tensor_reduce(sb["cnt32"][:, :], sb["gt01"][:, :, :], axis=AX.X, op=OP.add), selfwait=True)
            p.op("vector", dv, [],
                 lambda e, S: e.tensor_scalar(sb["cnt32"][:, :], sb["cnt32"][:, :], 7.5, None, op0=OP.is_ge), selfwait=True)
            nkv = p.op("vector", dv, [],
                       lambda e, S: e.tensor_scalar(sb["cnt32"][:, :], sb["cnt32"][:, :], -10000.0, None, op0=OP.mult), selfwait=True)

            for qt in range(NT):
                kts = list(range(8)) if qt < 2 else list(range(4 * qt + 4))
                pvv = None
                smv = None
                for kt in kts:
                    b = kt % 2
                    off = kt - 4 * qt
                    need_stair = (qt >= 2 and 1 <= off <= 3)
                    sv_ = p.op("tensor", mm, [(ld, lqv), p.guard(f"pb{b}")],
                               lambda e, S, kt=kt, qt=qt, b=b, need_stair=need_stair: e.matmul(
                                   pb[b][:, :], sb["kfm"][:, kt * 128:(kt + 1) * 128], sb["qfm"][:, qt * TT_:(qt + 1) * TT_],
                                   start=True, stop=(not need_stair)))
                    if need_stair:
                        sv_ = p.op("tensor", mm, [prep_dep],
                                   lambda e, S, b=b, off=off: e.matmul(pb[b][:, :], sb["identbf"][:, :], sb["stair"][:, off - 1, :], start=False, stop=True))
                    dmv = sv_
                    if qt >= 2 and 0 <= off <= 3:
                        qb = kt
                        ttv = p.op("tensor", mm, [(dv, nkv), p.guard("pb5")],
                                   lambda e, S, qb=qb: e.transpose(pb[5][0:1, 0:128], sb["cnt32"][:, qb:qb + 1], sb["ident32"][:, :]))
                        cpv = p.op("vector", dv, [(mm, ttv)],
                                   lambda e, S: e.tensor_copy(sb["nkrow"][0:1, 0:128], pb[5][0:1, 0:128]), selfwait=True)
                        kbv = p.op("tensor", mm, [(dv, cpv)],
                                   lambda e, S: e.matmul(pb[5][:, 0:128], sb["ones_bf"][0:1, :], sb["nkrow"][0:1, 0:128], start=True, stop=True))
                        p.set_guard("pb5", mm, kbv)
                        mtv = p.op("vector", dv, [(mm, kbv)],
                                   lambda e, S: e.tensor_tensor(sb["mtile"][:, :], sb["causal01"][:, :], pb[5][:, 0:128], op=OP.mult), selfwait=True)
                        p.set_guard("pb5", dv, mtv)
                        dmv = p.op("vector", dv, [(mm, sv_), (dv, mtv)],
                                   lambda e, S, b=b, off=off: e.tensor_tensor(pb[b][:, off * 128:(off + 1) * 128], pb[b][:, off * 128:(off + 1) * 128], sb["mtile"][:, :], op=OP.add))
                        exw = (dv, dmv)
                    else:
                        exw = (mm, sv_)
                    exv = p.op("scalar", sc, [exw, shift_dep, p.guard(f"ptl{b}")],
                               lambda e, S, b=b: e.activation(sb["ptl"][:, b, :], pb[b][:, :], AF.Exp, bias=sb["nrm"][:, 12:13]))
                    p.set_guard(f"pb{b}", sc, exv)
                    pvv = p.op("tensor", mm, [(sc, exv), (dv, msv), p.guard("pb2") if (kt == kts[0]) else None],
                               lambda e, S, kt=kt, b=b: e.matmul(pb[2][:, :], sb["vtm2"][:, kt, 0:128], sb["ptl"][:, b, :],
                                                                 start=(kt == kts[0]), stop=(kt == kts[-1])))
                    smv = p.op("tensor", mm, [(sc, exv), p.guard("pb3") if (kt == kts[0]) else None],
                               lambda e, S, kt=kt, b=b: e.matmul(pb[3][:, :], sb["ones_bf"][:, :], sb["ptl"][:, b, :],
                                                                 start=(kt == kts[0]), stop=(kt == kts[-1])))
                    p.set_guard(f"ptl{b}", mm, smv)
                rcv = p.op("vector", dv, [(mm, smv)],
                           lambda e, S: e.reciprocal(sb["rsum"][:, :], pb[3][:, :]))
                p.set_guard("pb3", dv, rcv)
                oev = p.op("vector", dv, [(mm, pvv), p.guard("ohb") if qt == 0 else None],
                           lambda e, S, qt=qt: e.tensor_tensor(sb["ohb"][:, qt * TT_:(qt + 1) * TT_], pb[2][:, :], sb["rsum"][:, :], op=OP.mult))
                p.set_guard("pb2", dv, oev)
            odv = p.val(dv)
            osv = p.dma("sync", tn["o1"][qh2], sb["ohb"][:, :], st, waits=[(dv, odv)])
            p.set_guard("qfm", mm, p.val(mm))
            p.set_guard("ohb", st, osv)
        p.set_guard("kfm", mm, p.val(mm))
        p.set_guard("vtm2", mm, p.val(mm))
        p.set_guard("vtm2ones", mm, p.val(mm))

    emit_matout(p, C, "s3o", tn["o1"], tn["wout1_c"], ("wout1_c", "wout1_h", 16, 2048), "R2", "R3")
